# version 10
# baseline (speedup 1.0000x reference)
import numpy as np
import concourse.bass as bass
import concourse.mybir as mybir
from concourse.bass_utils import run_bass_kernel_spmd

F32 = mybir.dt.float32
BF16 = mybir.dt.bfloat16
AF = mybir.ActivationFunctionType
ALU = mybir.AluOpType

NDIM = 1024
SEQ = 2048
HID = 4096
KC = NDIM // 128
HC = HID // 128
TA = 256
NT_A = SEQ // TA
TF = 512
NT_F = SEQ // TF
EPS = 1e-5

_CACHE = {}


def _dft_constants():
    if "tT_hi" in _CACHE:
        return
    N = NDIM
    e = np.arange(N)
    n = np.arange(N)
    j = np.arange(1, 512)
    Tp = np.zeros((N, N))
    Tp[0] = np.cos(np.pi * e)
    Tp[1:512] = np.cos(2 * np.pi * np.outer(j, e) / N)
    Tp[512] = 1.0
    Tp[513:] = -np.sin(2 * np.pi * np.outer(j, e) / N)
    iM = np.zeros((N, N))
    iM[0] = np.cos(np.pi * n) / N
    iM[1:512] = 2 * np.cos(2 * np.pi * np.outer(j, n) / N) / N
    iM[512] = 1.0 / N
    iM[513:] = -2 * np.sin(2 * np.pi * np.outer(j, n) / N) / N
    tT = np.ascontiguousarray(Tp.T)
    tT_hi = tT.astype(np.float32)
    tT_lo = (tT - tT_hi.astype(np.float64)).astype(np.float32)
    im_hi = iM.astype(np.float32)
    im_lo = (iM - im_hi.astype(np.float64)).astype(np.float32)
    _CACHE["tT_hi"], _CACHE["tT_lo"] = tT_hi, tT_lo
    _CACHE["im_hi"], _CACHE["im_lo"] = im_hi, im_lo


class _Trk:
    __slots__ = ("w", "rs")

    def __init__(self):
        self.w = None
        self.rs = []


class _Sched:
    ENGINES = ("sync", "gpsimd", "vector", "scalar", "tensor")
    NDMA = 12

    def __init__(self):
        self.ops = []
        self.counts = {"tns": 0, "vec": 0, "act": 0}
        for i in range(self.NDMA):
            self.counts[f"ds{i}"] = 0
            self.counts[f"dg{i}"] = 0
            self.counts[f"dc{i}"] = 0
        self.rr = {"sync": 0, "gpsimd": 0, "scalar": 0}
        self.fence_pending = {}

    def set_fence(self, fence):
        self.fence_pending = {e: list(fence) for e in self.ENGINES}

    def fence_now(self):
        return [(s, v, None, True) for s, v in self.counts.items() if v > 0]

    def op(self, eng, emit, reads=(), writes=(), extra_waits=(), dma=False,
           dma_sem=None):
        waits = {}
        def _add(tok):
            if tok is None:
                return
            sem, val, peng, pdma = tok
            if waits.get(sem, (0,))[0] < val:
                waits[sem] = (val,)
        for b in reads:
            _add(b.w)
        for b in writes:
            for t in b.rs:
                _add(t)
            _add(b.w)
        for t in extra_waits:
            _add(t)
        for t in self.fence_pending.pop(eng, ()):
            _add(t)
        if dma_sem is not None:
            sem = dma_sem
            amt = 16
            self.counts.setdefault(sem, 0)
        elif dma:
            i = self.rr[eng]
            self.rr[eng] = (i + 1) % self.NDMA
            sem = {"sync": "ds", "gpsimd": "dg", "scalar": "dc"}[eng] + str(i)
            amt = 16
            if self.counts[sem] > 0:
                _add((sem, self.counts[sem], None, True))
        else:
            sem = {"tensor": "tns", "vector": "vec", "scalar": "act"}[eng]
            amt = 1
        self.counts[sem] += amt
        tok = (sem, self.counts[sem], eng, dma)
        self.ops.append((eng, emit, [(s, v[0]) for s, v in waits.items()], sem, amt))
        for b in reads:
            b.rs.append(tok)
        for b in writes:
            b.w = tok
            b.rs = []
        return tok

    def emit_all(self, block, sems):
        def mk(eng_name):
            def run(h):
                watermark = {}
                for eng, emit, waits, sem, amt in self.ops:
                    if eng != eng_name:
                        continue
                    for sname, val in waits:
                        if watermark.get(sname, 0) < val:
                            h.wait_ge(sems[sname], val)
                            watermark[sname] = val
                    ins = emit(h)
                    ins.then_inc(sems[sem], amt)
            return run
        block.sync(mk("sync"))
        block.gpsimd(mk("gpsimd"))
        block.vector(mk("vector"))
        block.scalar(mk("scalar"))
        block.tensor(mk("tensor"))


def build_nc():
    nc = bass.Bass("TRN2", target_bir_lowering=False)
    S = _Sched()

    def din(name, shape, dt=F32):
        return nc.dram_tensor(name, shape, dt, kind="ExternalInput").ap()

    x_ds = din("x_ds", [NDIM, SEQ])
    worig = {w: din(f"w{w}o", [NDIM, NDIM]) for w in "qkv"}
    tT_hi = din("tT_hi", [NDIM, NDIM])
    tT_lo = din("tT_lo", [NDIM, NDIM])
    imr_hi = din("imr_hi", [NDIM, NDIM])
    imr_lo = din("imr_lo", [NDIM, NDIM])
    w1c = din("w1c", [KC, NDIM, 512])
    w2c = din("w2c", [KC, NDIM, 512])
    h1c = din("h1c", [HC, HID, 128])
    h2c = din("h2c", [HC, HID, 128])
    w3c = din("w3c", [KC, HID, 128])
    s1m = din("s1m", [128, HC])
    s2m = din("s2m", [128, HC])
    sh1m = din("sh1m", [128, HC])
    sh2m = din("sh2m", [128, HC])
    s3m = din("s3m", [128, KC])

    y = nc.dram_tensor("y", [NDIM, SEQ], F32, kind="ExternalOutput").ap()

    def scr(name, shape, dt):
        return nc.dram_tensor(name, shape, dt, kind="Internal").ap()

    xnh_scr = scr("xnh_scr", [NDIM, SEQ], BF16)
    xnl_scr = scr("xnl_scr", [NDIM, SEQ], BF16)
    part_scr = scr("part_scr", [NDIM, SEQ], F32)
    wl_scr = {w: scr(f"wl_{w}", [KC, NDIM, 2, 128], BF16) for w in "qkv"}
    w1b = scr("w1b", [KC, NDIM, 512], BF16)
    w2b = scr("w2b", [KC, NDIM, 512], BF16)
    h1b = scr("h1b", [HC, HID, 128], BF16)
    h2b = scr("h2b", [HC, HID, 128], BF16)
    w3b = scr("w3b", [KC, HID, 128], BF16)

    ones_col = nc.alloc_sbuf_tensor("ones_col", [128, 1], F32)
    ones_row = nc.alloc_sbuf_tensor("ones_row", [1, 128], F32)
    carry = nc.alloc_sbuf_tensor("carry", [128, 8], F32)
    zeros_sc = nc.alloc_sbuf_tensor("zeros_sc", [128, TA], F32)
    c_eps = nc.alloc_sbuf_tensor("c_eps", [128, 1], F32)
    c_nhalf = nc.alloc_sbuf_tensor("c_nhalf", [128, 1], F32)
    c_inv = nc.alloc_sbuf_tensor("c_inv", [128, 1], F32)
    s1c = nc.alloc_sbuf_tensor("s1c", [128, HC], F32)
    s2c = nc.alloc_sbuf_tensor("s2c", [128, HC], F32)
    sh1c = nc.alloc_sbuf_tensor("sh1c", [128, HC], F32)
    sh2c = nc.alloc_sbuf_tensor("sh2c", [128, HC], F32)
    s3c = nc.alloc_sbuf_tensor("s3c", [128, KC], F32)
    rowm = nc.alloc_sbuf_tensor("rowm", [1, TA], F32)
    rowq = nc.alloc_sbuf_tensor("rowq", [1, TA], F32)
    rowt = nc.alloc_sbuf_tensor("rowt", [1, TA], F32)
    rowv = nc.alloc_sbuf_tensor("rowv", [1, TA], F32)
    rowr = nc.alloc_sbuf_tensor("rowr", [1, TA], F32)
    rowmr = nc.alloc_sbuf_tensor("rowmr", [1, TA], F32)
    rowfix = nc.alloc_sbuf_tensor("rowfix", [1, TA], F32)

    ARENA_B = 198 * 1024
    arena = nc.alloc_sbuf_tensor("arena", [128, ARENA_B], mybir.dt.uint8)
    abase = nc.lookup_mloc(arena).addr

    offsets = {}

    def place(cursor, name, shape, dtype, align=128):
        nbytes = int(np.prod(shape[1:])) * mybir.dt.size(dtype)
        off = (cursor[0] + align - 1) // align * align
        assert off + nbytes <= ARENA_B, f"arena overflow at {name}: {off + nbytes}"
        cursor[0] = off + nbytes
        offsets[name] = off
        return nc.alloc_sbuf_tensor_at(name, shape, dtype, offset=abase + off)

    def place_at(name, at_name, shape, dtype):
        nbytes = int(np.prod(shape[1:])) * mybir.dt.size(dtype)
        return nc.alloc_sbuf_tensor_at(name, shape, dtype,
                                       offset=abase + offsets[at_name])

    ca = [0]
    T0 = place(ca, "T0", [128, KC, NDIM], BF16)
    T1 = place(ca, "T1", [128, KC, NDIM], BF16)
    Tl0 = place(ca, "Tl0", [128, KC, NDIM], BF16)
    arn_off = ca[0]
    WH = place(ca, "WHb", [128, KC, NDIM], BF16)
    WL = place(ca, "WLb", [128, KC, NDIM], BF16)
    IM0 = place(ca, "IM0b", [128, KC, NDIM], BF16)
    IM1 = place(ca, "IM1b", [128, KC, NDIM], BF16)
    IML = place(ca, "IMLb", [128, KC, NDIM], BF16)
    XTs = [place(ca, "XT0", [128, KC, TA], F32),
           place(ca, "XT1", [128, KC, TA], F32)]
    XNH = place(ca, "XNH", [128, KC, TA], BF16)
    XNL = place(ca, "XNL", [128, KC, TA], BF16)
    LH = place(ca, "LH", [128, KC, TA], BF16)
    LL = place(ca, "LL", [128, KC, TA], BF16)
    P0 = place(ca, "P0", [128, KC, TA], F32)
    P1 = place(ca, "P1", [128, KC, TA], F32)
    tmpA = place(ca, "tmpA", [128, 4, TA], F32)
    tmpB = place(ca, "tmpB", [128, 4, TA], F32)
    tmpC = place(ca, "tmpC", [128, 4, TA], F32)
    sqcs = [place(ca, "sqc0", [128, TA], F32),
            place(ca, "sqc1", [128, TA], F32)]
    xnc = place(ca, "xnc", [128, TA], F32)
    xnc2 = place(ca, "xnc2", [128, TA], F32)
    ptc0 = place(ca, "ptc0", [128, TA], F32)
    ptc1 = place(ca, "ptc1", [128, TA], F32)
    assert ca[0] <= ARENA_B
    NBOUNCE = 3
    wf32s = [place_at(f"wf32{i}", nm, [128, NDIM], F32)
             for i, nm in enumerate(("tmpA", "tmpB", "tmpC"))]
    SLA = [nc.alloc_sbuf_tensor_at(f"SLA{i}", [128, KC, 2, 128], BF16,
                                   offset=abase + offsets["WHb"] + 4096 * i)
           for i in range(3)]
    fold_stg = [nc.alloc_sbuf_tensor_at(f"fhl{i}", [128, 2, 512], BF16,
                                        offset=abase + offsets["XT1"] + 2048 * i)
                for i in range(4)]

    cf = [0]
    XNFHs = [place(cf, "XNFH0", [128, KC, TF], BF16),
             place(cf, "XNFH1", [128, KC, TF], BF16)]
    XNFLs = [place(cf, "XNFL0", [128, KC, TF], BF16),
             place(cf, "XNFL1", [128, KC, TF], BF16)]
    GH = place(cf, "GH", [128, HC, TF], BF16)
    GL = place(cf, "GL", [128, HC, TF], BF16)
    H1H = place(cf, "H1H", [128, HC, TF], BF16)
    H1L = place(cf, "H1L", [128, HC, TF], BF16)
    SLB0 = place(cf, "SLB0", [128, HC, 128], BF16)
    SLB1 = place(cf, "SLB1", [128, HC, 128], BF16)
    SLW0 = place_at("SLW0", "SLB0", [128, KC, 512], BF16)
    SLW1 = place_at("SLW1", "SLB1", [128, KC, 512], BF16)
    tsil = place(cf, "tsil", [128, TF], F32)
    g32 = place(cf, "g32", [128, TF], F32)
    h32 = tsil
    ptf = [place(cf, "ptf0", [128, TF], F32), place(cf, "ptf1", [128, TF], F32)]
    outc0 = place(cf, "outc0", [128, TF], F32)
    outc1 = place(cf, "outc1", [128, TF], F32)
    assert cf[0] <= ARENA_B, cf[0]
    H2H, H2L = GH, GL

    ps = [nc.alloc_psum_tensor(f"ps{i}", [128, 512], F32) for i in range(8)]

    trk = {}
    def T(obj, tag=None):
        key = (id(obj), tag)
        if key not in trk:
            trk[key] = _Trk()
        return trk[key]
    trk[(id(SLW0), None)] = T(SLB0)
    trk[(id(SLW1), None)] = T(SLB1)

    S.op("vector", lambda h: h.memset(ones_col[:], 1.0), writes=[T(ones_col)])
    S.op("vector", lambda h: h.memset(ones_row[:], 1.0), writes=[T(ones_row)])
    S.op("vector", lambda h: h.memset(carry[:], 0.0), writes=[T(carry)])
    S.op("vector", lambda h: h.memset(zeros_sc[:], 0.0), writes=[T(zeros_sc)])
    S.op("vector", lambda h: h.memset(c_eps[:], EPS), writes=[T(c_eps)])
    S.op("vector", lambda h: h.memset(c_nhalf[:], -0.5), writes=[T(c_nhalf)])
    S.op("vector", lambda h: h.memset(c_inv[:], 1.0 / NDIM), writes=[T(c_inv)])
    for dst, src in ((s1c, s1m), (s2c, s2m), (sh1c, sh1m), (sh2c, sh2m), (s3c, s3m)):
        S.op("sync", lambda h, d=dst, s=src: h.dma_start(out=d[:], in_=s),
             writes=[T(dst)], dma=True)

    def emit_ln(t):
        t0 = t * TA
        XT = XTs[t % 2]

        S.op("sync", lambda h, o=t0, X=XT: h.dma_start(
            out=X[:], in_=x_ds[:, o:o + TA].rearrange("(c p) t -> p c t", p=128)),
            writes=[T(XT)], dma=True)

        def mm_sum(h, c, X=XT):
            return h.matmul(ps[4][0:1, 0:TA], ones_col[:], X[:, c, :],
                            start=(c == 0), stop=(c == KC - 1))
        for c in range(KC):
            S.op("tensor", lambda h, c=c, f=mm_sum: f(h, c),
                 reads=[T(XT), T(ones_col)], writes=[T(ps[4])])
        for c in range(KC):
            sqc = sqcs[c % 2]
            S.op("scalar", lambda h, c=c, q=sqc, X=XT: h.activation(q[:], X[:, c, :], AF.Square),
                 reads=[T(XT)], writes=[T(sqc)])
            S.op("tensor", lambda h, c=c, q=sqc: h.matmul(
                ps[5][0:1, 0:TA], ones_col[:], q[:],
                start=(c == 0), stop=(c == KC - 1)),
                reads=[T(sqc), T(ones_col)], writes=[T(ps[5])])

        S.op("scalar", lambda h: h.activation(rowm[:], ps[4][0:1, 0:TA], AF.Copy,
                                              scale=c_inv[0:1, 0:1]),
             reads=[T(ps[4]), T(c_inv)], writes=[T(rowm)])
        S.op("scalar", lambda h: h.activation(rowq[:], ps[5][0:1, 0:TA], AF.Copy,
                                              scale=c_inv[0:1, 0:1]),
             reads=[T(ps[5]), T(c_inv)], writes=[T(rowq)])
        S.op("vector", lambda h: h.tensor_tensor(rowt[:], rowm[:], rowm[:], ALU.mult),
             reads=[T(rowm)], writes=[T(rowt)])
        S.op("vector", lambda h: h.tensor_tensor(rowv[:], rowq[:], rowt[:], ALU.subtract),
             reads=[T(rowq), T(rowt)], writes=[T(rowv)])
        S.op("scalar", lambda h: h.activation(rowt[:], rowv[:], AF.Ln,
                                              bias=c_eps[0:1, 0:1]),
             reads=[T(rowv), T(c_eps)], writes=[T(rowt)])
        S.op("scalar", lambda h: h.activation(rowr[:], rowt[:], AF.Exp,
                                              scale=c_nhalf[0:1, 0:1]),
             reads=[T(rowt), T(c_nhalf)], writes=[T(rowr)])
        S.op("vector", lambda h: h.tensor_tensor(rowmr[:], rowm[:], rowr[:], ALU.mult),
             reads=[T(rowm), T(rowr)], writes=[T(rowmr)])

        S.op("tensor", lambda h: h.matmul(ps[6][:, 0:TA], ones_row[:], rowr[:],
                                          start=True, stop=True),
             reads=[T(rowr), T(ones_row)], writes=[T(ps[6])])
        S.op("tensor", lambda h: h.matmul(ps[7][:, 0:TA], ones_row[:], rowmr[:],
                                          start=True, stop=True),
             reads=[T(rowmr), T(ones_row)], writes=[T(ps[7])])

        for c in range(KC):
            S.op("vector", lambda h, c=c, X=XT: h.tensor_tensor(
                xnc[:], X[:, c, :], ps[6][:, 0:TA], ALU.mult),
                reads=[T(XT), T(ps[6])], writes=[T(xnc)])
            S.op("vector", lambda h, c=c: h.tensor_tensor(
                xnc2[:], xnc[:], ps[7][:, 0:TA], ALU.subtract),
                reads=[T(xnc), T(ps[7])], writes=[T(xnc2)])
            S.op("scalar", lambda h, c=c: h.copy(XNH[:, c, :], xnc2[:]),
                 reads=[T(xnc2)], writes=[T(XNH)])
            S.op("vector", lambda h, c=c: h.tensor_tensor(
                XNL[:, c, :], xnc2[:], XNH[:, c, :], ALU.subtract),
                reads=[T(xnc2), T(XNH)], writes=[T(XNL)])
        S.op("gpsimd", lambda h, o=t0: h.dma_start(
            out=xnh_scr[:, o:o + TA].rearrange("(c p) t -> p c t", p=128), in_=XNH[:]),
            reads=[T(XNH)], writes=[T(xnh_scr)], dma=True)
        S.op("gpsimd", lambda h, o=t0: h.dma_start(
            out=xnl_scr[:, o:o + TA].rearrange("(c p) t -> p c t", p=128), in_=XNL[:]),
            reads=[T(XNL)], writes=[T(xnl_scr)], dma=True)

    emit_ln(0)

    wjobs = []
    for kc in range(KC):
        wjobs.append(("res2", tT_hi[kc * 128:(kc + 1) * 128, :], T0, T1, kc))
    for kc in range(KC):
        wjobs.append(("res1", tT_lo[kc * 128:(kc + 1) * 128, :], Tl0, None, kc))
    for kc in range(KC):
        wjobs.append(("res2", imr_hi[kc * 128:(kc + 1) * 128, :], IM0, IM1, kc))
    for kc in range(KC):
        wjobs.append(("res1", imr_lo[kc * 128:(kc + 1) * 128, :], IML, None, kc))
    def emit_res_jobs(jobs):
        for job in jobs:
            kind, src, d0, d1, kc = job
            i = emit_res_jobs.wi % NBOUNCE
            emit_res_jobs.wi += 1
            wf32 = wf32s[i]
            S.op("sync", lambda h, s=src, b=wf32: h.dma_start(out=b[:], in_=s),
                 writes=[T(wf32)], dma=True)
            S.op("scalar", lambda h, b=wf32, o=d0, k=kc: h.copy(o[:, k, :], b[:]),
                 reads=[T(wf32)], writes=[T(d0)])
            if kind == "res2":
                S.op("vector", lambda h, b=wf32, o0=d0, o1=d1, k=kc: h.tensor_tensor(
                    o1[:, k, :], b[:], o0[:, k, :], ALU.subtract),
                    reads=[T(wf32), T(d0)], writes=[T(d1)])
    emit_res_jobs.wi = 0
    emit_res_jobs(wjobs)

    for w in "kvq":
        for kc in range(KC):
            i = emit_res_jobs.wi % NBOUNCE
            emit_res_jobs.wi += 1
            wf32 = wf32s[i]
            S.op("sync", lambda h, s=worig[w], k=kc, b=wf32: h.dma_start(
                out=b[:], in_=s[k * 128:(k + 1) * 128, :]),
                writes=[T(wf32)], dma=True)
            S.op("scalar", lambda h, b=wf32, k=kc: h.copy(WH[:, k, :], b[:]),
                 reads=[T(wf32)], writes=[T(WH)])
            S.op("vector", lambda h, b=wf32, k=kc: h.tensor_tensor(
                WL[:, k, :], b[:], WH[:, k, :], ALU.subtract),
                reads=[T(wf32), T(WH)], writes=[T(WL)])
        for dc in range(KC):
            for half in range(2):
                bank = ps[(2 * dc + half) % 4]
                def mm_fold(h, dc=dc, half=half, bank=bank):
                    ins = None
                    first = True
                    for la, lb in ((WH, T0), (WL, T0), (WH, T1), (WH, Tl0)):
                        for kc2 in range(KC):
                            ins = h.matmul(
                                bank[:],
                                la[:, kc2, dc * 128:(dc + 1) * 128],
                                lb[:, kc2, half * 512:half * 512 + 512],
                                start=first,
                                stop=(la is WH and lb is Tl0 and kc2 == KC - 1))
                            first = False
                    return ins
                S.op("tensor", mm_fold,
                     reads=[T(WH), T(WL), T(T0), T(T1), T(Tl0)], writes=[T(bank)])
                fhl = fold_stg[(2 * dc + half) % 4]
                S.op("scalar", lambda h, bank=bank, o=fhl: h.copy(o[:, 0, :], bank[:]),
                     reads=[T(bank)], writes=[T(fhl, "hi")])
                S.op("vector", lambda h, bank=bank, o=fhl: h.tensor_tensor(
                    o[:, 1, :], bank[:], o[:, 0, :], ALU.subtract),
                    reads=[T(bank), T(fhl, "hi")], writes=[T(fhl, "lo")])
                S.op("gpsimd", lambda h, d=wl_scr[w], dc=dc, half=half, o=fhl: h.dma_start(
                    out=d[4 * half:4 * half + 4, dc * 128:(dc + 1) * 128, :, :]
                        .rearrange("c d l m -> d l c m"),
                    in_=o[:].rearrange("d l (c m) -> d l c m", c=4)),
                    reads=[T(fhl, "hi"), T(fhl, "lo")], writes=[T(wl_scr[w])], dma=True)

    S.set_fence(S.fence_now())

    for name, src, dst, nch, inner in (
            ("cw1", w1c, w1b, KC, NDIM * 512), ("cw2", w2c, w2b, KC, NDIM * 512),
            ("ch1", h1c, h1b, HC, HID * 128), ("ch2", h2c, h2b, HC, HID * 128),
            ("cw3", w3c, w3b, KC, HID * 128)):
        step = max(1, (4 << 20) // (inner * 4))
        for c0 in range(0, nch, step):
            c1 = min(nch, c0 + step)
            S.op("gpsimd",
                 lambda h, s=src, d=dst, a=c0, b=c1: h.dma_start(out=d[a:b], in_=s[a:b]),
                 writes=[T(dst)], dma=True, dma_sem=name)

    for t in range(NT_A):
        t0 = t * TA
        XT = XTs[t % 2]

        def proj(wname, dstplane):
            for rc in range(KC):
                sl = SLA[rc % 3]
                S.op("scalar", lambda h, s=wl_scr[wname], rc=rc, d=sl: h.dma_start(
                    out=d[:], in_=s[rc].rearrange("(c p) l m -> p c l m", p=128)),
                    reads=[T(wl_scr[wname])], writes=[T(sl)], dma=True)
                bank = ps[rc % 4]
                def mm_proj(h, rc=rc, bank=bank, sl=sl):
                    ins = None
                    for li, lb in ((0, XNH), (0, XNL), (1, XNH)):
                        for kc2 in range(KC):
                            ins = h.matmul(
                                bank[:, 0:TA],
                                sl[:, kc2, li, :],
                                lb[:, kc2, :],
                                start=(li == 0 and lb is XNH and kc2 == 0),
                                stop=(li == 1 and kc2 == KC - 1))
                    return ins
                S.op("tensor", mm_proj,
                     reads=[T(sl), T(XNH), T(XNL)], writes=[T(bank)])
                S.op("scalar", lambda h, rc=rc, bank=bank, d=dstplane: h.copy(
                    d[:, rc, :], bank[:, 0:TA]),
                    reads=[T(bank)], writes=[T(dstplane)])

        proj("k", P0)
        proj("v", P1)

        RE = (slice(None), slice(0, 4), slice(None))
        IM = (slice(None), slice(4, 8), slice(None))
        S.op("vector", lambda h: h.tensor_tensor(
            rowfix[:], P0[0:1, 4, :], P1[0:1, 4, :], ALU.mult),
            reads=[T(P0), T(P1)], writes=[T(rowfix)])
        S.op("vector", lambda h: h.tensor_tensor(tmpA[:], P0[RE], P1[RE], ALU.mult),
             reads=[T(P0), T(P1)], writes=[T(tmpA)])
        S.op("vector", lambda h: h.tensor_tensor(tmpB[:], P0[IM], P1[IM], ALU.mult),
             reads=[T(P0), T(P1)], writes=[T(tmpB)])
        S.op("vector", lambda h: h.tensor_tensor(tmpC[:], tmpA[:], tmpB[:], ALU.subtract),
             reads=[T(tmpA), T(tmpB)], writes=[T(tmpC)])
        S.op("vector", lambda h: h.tensor_tensor(
            tmpC[0:1, 0, :], P0[0:1, 0, :], P1[0:1, 0, :], ALU.mult),
            reads=[T(P0), T(P1)], writes=[T(tmpC)])
        S.op("vector", lambda h: h.tensor_tensor(tmpA[:], P0[RE], P1[IM], ALU.mult),
             reads=[T(P0), T(P1)], writes=[T(tmpA)])
        S.op("vector", lambda h: h.tensor_tensor(tmpB[:], P0[IM], P1[RE], ALU.mult),
             reads=[T(P0), T(P1)], writes=[T(tmpB)])
        S.op("vector", lambda h: h.tensor_tensor(P0[IM], tmpA[:], tmpB[:], ALU.add),
             reads=[T(tmpA), T(tmpB)], writes=[T(P0)])
        S.op("scalar", lambda h: h.copy(P0[0:1, 4, :], rowfix[:]),
             reads=[T(rowfix)], writes=[T(P0)])
        for c in range(4):
            S.op("vector", lambda h, c=c: h.tensor_tensor_scan(
                P1[:, c, :], tmpC[:, c, :], zeros_sc[:], carry[:, c:c + 1],
                ALU.add, ALU.add),
                reads=[T(tmpC), T(zeros_sc), T(carry)], writes=[T(P1)])
        for c in range(4):
            S.op("vector", lambda h, c=c: h.tensor_tensor_scan(
                P1[:, 4 + c, :], P0[:, 4 + c, :], zeros_sc[:], carry[:, 4 + c:5 + c],
                ALU.add, ALU.add),
                reads=[T(P0), T(zeros_sc), T(carry)], writes=[T(P1)])
        S.op("vector", lambda h: h.tensor_copy(carry[:, 0:KC], P1[:, :, TA - 1]),
             reads=[T(P1)], writes=[T(carry)])

        proj("q", P0)

        S.op("vector", lambda h: h.tensor_tensor(
            rowfix[:], P1[0:1, 4, :], P0[0:1, 4, :], ALU.mult),
            reads=[T(P0), T(P1)], writes=[T(rowfix)])
        S.op("vector", lambda h: h.tensor_tensor(tmpA[:], P1[RE], P0[RE], ALU.mult),
             reads=[T(P0), T(P1)], writes=[T(tmpA)])
        S.op("vector", lambda h: h.tensor_tensor(tmpB[:], P1[IM], P0[IM], ALU.mult),
             reads=[T(P0), T(P1)], writes=[T(tmpB)])
        S.op("vector", lambda h: h.tensor_tensor(tmpC[:], tmpA[:], tmpB[:], ALU.add),
             reads=[T(tmpA), T(tmpB)], writes=[T(tmpC)])
        S.op("vector", lambda h: h.tensor_tensor(
            tmpC[0:1, 0, :], P1[0:1, 0, :], P0[0:1, 0, :], ALU.mult),
            reads=[T(P0), T(P1)], writes=[T(tmpC)])
        S.op("vector", lambda h: h.tensor_tensor(tmpA[:], P1[IM], P0[RE], ALU.mult),
             reads=[T(P0), T(P1)], writes=[T(tmpA)])
        S.op("vector", lambda h: h.tensor_tensor(tmpB[:], P1[RE], P0[IM], ALU.mult),
             reads=[T(P0), T(P1)], writes=[T(tmpB)])
        S.op("vector", lambda h: h.tensor_tensor(P1[IM], tmpA[:], tmpB[:], ALU.subtract),
             reads=[T(tmpA), T(tmpB)], writes=[T(P1)])
        S.op("scalar", lambda h: h.copy(P1[0:1, 4, :], rowfix[:]),
             reads=[T(rowfix)], writes=[T(P1)])

        S.op("scalar", lambda h: h.copy(LH[:, 0:4, :], tmpC[:]),
             reads=[T(tmpC)], writes=[T(LH, "lo")])
        S.op("vector", lambda h: h.tensor_tensor(
            LL[:, 0:4, :], tmpC[:], LH[:, 0:4, :], ALU.subtract),
            reads=[T(tmpC), T(LH, "lo")], writes=[T(LL, "lo")])
        S.op("scalar", lambda h: h.copy(LH[:, 4:8, :], P1[IM]),
             reads=[T(P1)], writes=[T(LH, "hi")])
        S.op("vector", lambda h: h.tensor_tensor(
            LL[:, 4:8, :], P1[IM], LH[:, 4:8, :], ALU.subtract),
            reads=[T(P1), T(LH, "hi")], writes=[T(LL, "hi")])

        if t + 1 < NT_A:
            emit_ln(t + 1)

        for dc in range(KC):
            bank = ps[dc % 4]
            def mm_inv_half(h, dc=dc, bank=bank, klo=0, khi=4, first=True, last=False):
                ins = None
                for la, lb in ((IM0, LH), (IM1, LH), (IM0, LL), (IML, LH)):
                    for kc2 in range(klo, khi):
                        ins = h.matmul(
                            bank[:, 0:TA],
                            la[:, kc2, dc * 128:(dc + 1) * 128],
                            lb[:, kc2, :],
                            start=first,
                            stop=(last and la is IML and kc2 == khi - 1))
                        first = False
                return ins
            import functools as _ft
            S.op("tensor", _ft.partial(mm_inv_half, klo=0, khi=4, first=True, last=False),
                 reads=[T(IM0), T(IM1), T(IML), T(LH, "lo"), T(LL, "lo")],
                 writes=[T(bank)])
            S.op("tensor", _ft.partial(mm_inv_half, klo=4, khi=8, first=False, last=True),
                 reads=[T(IM0), T(IM1), T(IML), T(LH, "hi"), T(LL, "hi")],
                 writes=[T(bank)])
            pt = ptc0 if dc % 2 == 0 else ptc1
            S.op("vector", lambda h, dc=dc, bank=bank, pt=pt, X=XT: h.tensor_tensor(
                pt[:], bank[:, 0:TA], X[:, dc, :], ALU.add),
                reads=[T(bank), T(XT)], writes=[T(pt)])
            S.op("gpsimd", lambda h, dc=dc, o=t0, pt=pt: h.dma_start(
                out=part_scr[dc * 128:(dc + 1) * 128, o:o + TA], in_=pt[:]),
                reads=[T(pt)], writes=[T(part_scr)], dma=True)

    S.set_fence(S.fence_now())
    for t in range(NT_F):
        t0 = t * TF
        XNFH, XNFL = XNFHs[t % 2], XNFLs[t % 2]
        S.op("sync", lambda h, o=t0, D=XNFH: h.dma_start(
            out=D[:], in_=xnh_scr[:, o:o + TF].rearrange("(c p) t -> p c t", p=128)),
            reads=[T(xnh_scr)], writes=[T(XNFH)], dma=True)
        S.op("sync", lambda h, o=t0, D=XNFL: h.dma_start(
            out=D[:], in_=xnl_scr[:, o:o + TF].rearrange("(c p) t -> p c t", p=128)),
            reads=[T(xnl_scr)], writes=[T(XNFL)], dma=True)

        for oq in range(KC):
            S.op("sync", lambda h, oq=oq: h.dma_start(
                out=SLW0[:], in_=w1b[oq].rearrange("(c p) m -> p c m", p=128)),
                reads=[T(w1b)], writes=[T(SLW0)], dma=True)
            S.op("sync", lambda h, oq=oq: h.dma_start(
                out=SLW1[:], in_=w2b[oq].rearrange("(c p) m -> p c m", p=128)),
                reads=[T(w2b)], writes=[T(SLW1)], dma=True)
            for ol in range(4):
                oc = oq * 4 + ol
                b1 = ps[(2 * oc) % 4]
                b2 = ps[(2 * oc + 1) % 4]
                def mm_w12(h, ol=ol, sl=None, bank=None, XH=XNFH, XL=XNFL):
                    ins = None
                    first = True
                    for lb in (XH, XL):
                        for kc2 in range(KC):
                            ins = h.matmul(bank[:, 0:TF],
                                           sl[:, kc2, ol * 128:(ol + 1) * 128],
                                           lb[:, kc2, :],
                                           start=first,
                                           stop=(lb is XL and kc2 == KC - 1))
                            first = False
                    return ins
                import functools
                S.op("tensor", functools.partial(mm_w12, sl=SLW0, bank=b1),
                     reads=[T(SLW0), T(XNFH), T(XNFL)], writes=[T(b1)])
                S.op("tensor", functools.partial(mm_w12, sl=SLW1, bank=b2),
                     reads=[T(SLW1), T(XNFH), T(XNFL)], writes=[T(b2)])
                S.op("scalar", lambda h, oc=oc, bank=b1: h.activation(
                    tsil[:], bank[:, 0:TF], AF.Silu, scale=s1c[:, oc:oc + 1]),
                    reads=[T(b1), T(s1c)], writes=[T(tsil)])
                S.op("vector", lambda h, oc=oc, bank=b2: h.scalar_tensor_tensor(
                    g32[:], bank[:, 0:TF], s2c[:, oc:oc + 1], tsil[:],
                    ALU.mult, ALU.mult),
                    reads=[T(b2), T(s2c), T(tsil)], writes=[T(g32)])
                S.op("vector", lambda h, oc=oc: h.tensor_copy(GH[:, oc, :], g32[:]),
                     reads=[T(g32)], writes=[T(GH)])
                S.op("vector", lambda h, oc=oc: h.tensor_tensor(
                    GL[:, oc, :], g32[:], GH[:, oc, :], ALU.subtract),
                    reads=[T(g32), T(GH)], writes=[T(GL)])

        def hidden_layer(wsrc, sc_col, inH, inL, outH, outL):
            for oc in range(HC):
                sl = SLB0 if oc % 2 == 0 else SLB1
                bank = ps[oc % 4]
                S.op("sync", lambda h, oc=oc, sl=sl, w=wsrc: h.dma_start(
                    out=sl[:], in_=w[oc].rearrange("(c p) m -> p c m", p=128)),
                    reads=[T(wsrc)], writes=[T(sl)], dma=True)
                def mm_h(h, oc=oc, sl=sl, bank=bank, inH=inH, inL=inL):
                    ins = None
                    first = True
                    for lb in (inH, inL):
                        for kc2 in range(HC):
                            ins = h.matmul(bank[:, 0:TF], sl[:, kc2, :], lb[:, kc2, :],
                                           start=first,
                                           stop=(lb is inL and kc2 == HC - 1))
                            first = False
                    return ins
                S.op("tensor", mm_h, reads=[T(sl), T(inH), T(inL)], writes=[T(bank)])
                S.op("scalar", lambda h, oc=oc, bank=bank, sc=sc_col: h.activation(
                    h32[:], bank[:, 0:TF], AF.Gelu, scale=sc[:, oc:oc + 1]),
                    reads=[T(bank), T(sc_col)], writes=[T(h32)])
                S.op("vector", lambda h, oc=oc, oh=outH: h.tensor_copy(
                    oh[:, oc, :], h32[:]),
                    reads=[T(h32)], writes=[T(outH)])
                S.op("vector", lambda h, oc=oc, oh=outH, ol=outL: h.tensor_tensor(
                    ol[:, oc, :], h32[:], oh[:, oc, :], ALU.subtract),
                    reads=[T(h32), T(outH)], writes=[T(outL)])

        hidden_layer(h1b, sh1c, GH, GL, H1H, H1L)
        hidden_layer(h2b, sh2c, H1H, H1L, H2H, H2L)

        for dc in range(KC):
            sl = SLB0 if dc % 2 == 0 else SLB1
            bank = ps[dc % 4]
            S.op("sync", lambda h, dc=dc, sl=sl: h.dma_start(
                out=sl[:], in_=w3b[dc].rearrange("(c p) m -> p c m", p=128)),
                reads=[T(w3b)], writes=[T(sl)], dma=True)
            def mm_w3(h, dc=dc, sl=sl, bank=bank):
                ins = None
                first = True
                for lb in (H2H, H2L):
                    for kc2 in range(HC):
                        ins = h.matmul(bank[:, 0:TF], sl[:, kc2, :], lb[:, kc2, :],
                                       start=first, stop=(lb is H2L and kc2 == HC - 1))
                        first = False
                return ins
            S.op("tensor", mm_w3, reads=[T(sl), T(H2H), T(H2L)], writes=[T(bank)])
            ptb = ptf[dc % 2]
            S.op("sync", lambda h, dc=dc, o=t0, D=ptb: h.dma_start(
                out=D[:], in_=part_scr[dc * 128:(dc + 1) * 128, o:o + TF]),
                reads=[T(part_scr)], writes=[T(ptb)], dma=True)
            oc_buf = outc0 if dc % 2 == 0 else outc1
            S.op("vector", lambda h, dc=dc, bank=bank, ob=oc_buf, D=ptb: h.scalar_tensor_tensor(
                ob[:], bank[:, 0:TF], s3c[:, dc:dc + 1], D[:],
                ALU.mult, ALU.add),
                reads=[T(bank), T(s3c), T(ptb)], writes=[T(oc_buf)])
            S.op("gpsimd", lambda h, dc=dc, o=t0, ob=oc_buf: h.dma_start(
                out=y[dc * 128:(dc + 1) * 128, o:o + TF], in_=ob[:]),
                reads=[T(oc_buf)], writes=[], dma=True)

    finals = dict(S.counts)

    import contextlib
    with contextlib.ExitStack() as stack:
        sems = {name: stack.enter_context(nc.semaphore("sem_" + name))
                for name in S.counts}
        block = stack.enter_context(nc.Block())
        S.emit_all(block, sems)

        @block.sync
        def _(sync):
            for name, val in finals.items():
                if val > 0:
                    sync.wait_ge(sems[name], val)

    return nc


def _prep_inputs(inputs):
    _dft_constants()
    x = np.asarray(inputs["L0_attn_x"])
    com = {
        "tT_hi": _CACHE["tT_hi"], "tT_lo": _CACHE["tT_lo"],
        "imr_hi": _CACHE["im_hi"], "imr_lo": _CACHE["im_lo"],
    }
    for w in "qkv":
        com[f"w{w}o"] = np.ascontiguousarray(np.asarray(inputs[f"L0_attn_W{w}"]))
    def chunked(a, cw=128):
        O, K = a.shape
        return np.ascontiguousarray(
            a.T.reshape(K, O // cw, cw).transpose(1, 0, 2))
    com["w1c"] = chunked(np.asarray(inputs["L0_ffn_w1_w"]), 512)
    com["w2c"] = chunked(np.asarray(inputs["L0_ffn_w2_w"]), 512)
    com["h1c"] = chunked(np.asarray(inputs["L0_ffn_h1_w"]))
    com["h2c"] = chunked(np.asarray(inputs["L0_ffn_h2_w"]))
    com["w3c"] = chunked(np.asarray(inputs["L0_ffn_w3_w"]))
    def colmaj(v):
        v = np.asarray(v)
        return np.ascontiguousarray(v.reshape(-1, 128).T)
    com["s1m"] = colmaj(inputs["L0_ffn_w1_s"])
    com["s2m"] = colmaj(inputs["L0_ffn_w2_s"])
    com["sh1m"] = colmaj(inputs["L0_ffn_h1_s"])
    com["sh2m"] = colmaj(inputs["L0_ffn_h2_s"])
    com["s3m"] = colmaj(inputs["L0_ffn_w3_s"])
    in_maps = []
    for b in range(8):
        m = dict(com)
        m["x_ds"] = np.ascontiguousarray(x[b].T)
        in_maps.append(m)
    return in_maps


def kernel(**inputs):
    in_maps = _prep_inputs(inputs)
    if "nc" not in _CACHE:
        _CACHE["nc"] = build_nc()
    res = run_bass_kernel_spmd(_CACHE["nc"], in_maps, core_ids=list(range(8)))
    out = np.stack([np.asarray(r["y"]).T for r in res.results], axis=0)
    return np.ascontiguousarray(out.astype(np.float32, copy=False))


# version 11
# speedup vs baseline: 1.1952x; 1.1952x over previous
import numpy as np
import concourse.bass as bass
import concourse.mybir as mybir
from concourse.bass_utils import run_bass_kernel_spmd

F32 = mybir.dt.float32
BF16 = mybir.dt.bfloat16
AF = mybir.ActivationFunctionType
ALU = mybir.AluOpType

NDIM = 1024
SEQ = 2048
HID = 4096
KC = NDIM // 128
HC = HID // 128
TA = 256
NT_A = SEQ // TA
TF = 512
NT_F = SEQ // TF
EPS = 1e-5

_CACHE = {}


def _dft_constants():
    if "tT_hi" in _CACHE:
        return
    N = NDIM
    e = np.arange(N)
    n = np.arange(N)
    j = np.arange(1, 512)
    Tp = np.zeros((N, N))
    Tp[0] = np.cos(np.pi * e)
    Tp[1:512] = np.cos(2 * np.pi * np.outer(j, e) / N)
    Tp[512] = 1.0
    Tp[513:] = -np.sin(2 * np.pi * np.outer(j, e) / N)
    iM = np.zeros((N, N))
    iM[0] = np.cos(np.pi * n) / N
    iM[1:512] = 2 * np.cos(2 * np.pi * np.outer(j, n) / N) / N
    iM[512] = 1.0 / N
    iM[513:] = -2 * np.sin(2 * np.pi * np.outer(j, n) / N) / N
    tT = np.ascontiguousarray(Tp.T)
    tT_hi = tT.astype(np.float32)
    tT_lo = (tT - tT_hi.astype(np.float64)).astype(np.float32)
    im_hi = iM.astype(np.float32)
    im_lo = (iM - im_hi.astype(np.float64)).astype(np.float32)
    _CACHE["tT_hi"], _CACHE["tT_lo"] = tT_hi, tT_lo
    _CACHE["im_hi"], _CACHE["im_lo"] = im_hi, im_lo


class _Trk:
    __slots__ = ("w", "rs")

    def __init__(self):
        self.w = None
        self.rs = []


class _Sched:
    ENGINES = ("sync", "gpsimd", "vector", "scalar", "tensor")
    NDMA = 12

    def __init__(self):
        self.ops = []
        self.counts = {"tns": 0, "vec": 0, "act": 0}
        for i in range(self.NDMA):
            self.counts[f"ds{i}"] = 0
            self.counts[f"dg{i}"] = 0
            self.counts[f"dc{i}"] = 0
        self.rr = {"sync": 0, "gpsimd": 0, "scalar": 0}
        self.fence_pending = {}

    def set_fence(self, fence):
        self.fence_pending = {e: list(fence) for e in self.ENGINES}

    def fence_now(self):
        return [(s, v, None, True) for s, v in self.counts.items() if v > 0]

    def op(self, eng, emit, reads=(), writes=(), extra_waits=(), dma=False,
           dma_sem=None):
        waits = {}
        def _add(tok):
            if tok is None:
                return
            sem, val, peng, pdma = tok
            if waits.get(sem, (0,))[0] < val:
                waits[sem] = (val,)
        for b in reads:
            _add(b.w)
        for b in writes:
            for t in b.rs:
                _add(t)
            _add(b.w)
        for t in extra_waits:
            _add(t)
        for t in self.fence_pending.pop(eng, ()):
            _add(t)
        if dma_sem is not None:
            sem = dma_sem
            amt = 16
            self.counts.setdefault(sem, 0)
        elif dma:
            i = self.rr[eng]
            self.rr[eng] = (i + 1) % self.NDMA
            sem = {"sync": "ds", "gpsimd": "dg", "scalar": "dc"}[eng] + str(i)
            amt = 16
            if self.counts[sem] > 0:
                _add((sem, self.counts[sem], None, True))
        else:
            sem = {"tensor": "tns", "vector": "vec", "scalar": "act"}[eng]
            amt = 1
        self.counts[sem] += amt
        tok = (sem, self.counts[sem], eng, dma)
        self.ops.append((eng, emit, [(s, v[0]) for s, v in waits.items()], sem, amt))
        for b in reads:
            b.rs.append(tok)
        for b in writes:
            b.w = tok
            b.rs = []
        return tok

    def emit_all(self, block, sems):
        def mk(eng_name):
            def run(h):
                watermark = {}
                for eng, emit, waits, sem, amt in self.ops:
                    if eng != eng_name:
                        continue
                    for sname, val in waits:
                        if watermark.get(sname, 0) < val:
                            h.wait_ge(sems[sname], val)
                            watermark[sname] = val
                    ins = emit(h)
                    ins.then_inc(sems[sem], amt)
            return run
        block.sync(mk("sync"))
        block.gpsimd(mk("gpsimd"))
        block.vector(mk("vector"))
        block.scalar(mk("scalar"))
        block.tensor(mk("tensor"))


def build_nc():
    nc = bass.Bass("TRN2", target_bir_lowering=False)
    S = _Sched()

    def din(name, shape, dt=F32):
        return nc.dram_tensor(name, shape, dt, kind="ExternalInput").ap()

    x_ds = din("x_ds", [NDIM, SEQ])
    worig = {w: din(f"w{w}o", [NDIM, NDIM]) for w in "qkv"}
    tT_hi = din("tT_hi", [NDIM, NDIM])
    tT_lo = din("tT_lo", [NDIM, NDIM])
    imr_hi = din("imr_hi", [NDIM, NDIM])
    imr_lo = din("imr_lo", [NDIM, NDIM])
    w1c = din("w1c", [KC, NDIM, 512])
    w2c = din("w2c", [KC, NDIM, 512])
    h1c = din("h1c", [HC, HID, 128])
    h2c = din("h2c", [HC, HID, 128])
    w3c = din("w3c", [KC, HID, 128])
    s1m = din("s1m", [128, HC])
    s2m = din("s2m", [128, HC])
    sh1m = din("sh1m", [128, HC])
    sh2m = din("sh2m", [128, HC])
    s3m = din("s3m", [128, KC])

    y = nc.dram_tensor("y", [NDIM, SEQ], F32, kind="ExternalOutput").ap()

    def scr(name, shape, dt):
        return nc.dram_tensor(name, shape, dt, kind="Internal").ap()

    xnh_scr = scr("xnh_scr", [NDIM, SEQ], BF16)
    xnl_scr = scr("xnl_scr", [NDIM, SEQ], BF16)
    part_scr = scr("part_scr", [NDIM, SEQ], F32)
    wl_scr = {w: scr(f"wl_{w}", [KC, NDIM, 2, 128], BF16) for w in "qkv"}
    w1b = scr("w1b", [KC, NDIM, 512], BF16)
    w2b = scr("w2b", [KC, NDIM, 512], BF16)
    h1b = scr("h1b", [HC, HID, 128], BF16)
    h2b = scr("h2b", [HC, HID, 128], BF16)
    w3b = scr("w3b", [KC, HID, 128], BF16)

    ones_col = nc.alloc_sbuf_tensor("ones_col", [128, 1], F32)
    ones_row = nc.alloc_sbuf_tensor("ones_row", [1, 128], F32)
    carry = nc.alloc_sbuf_tensor("carry", [128, 8], F32)
    zeros_sc = nc.alloc_sbuf_tensor("zeros_sc", [128, TA], F32)
    c_eps = nc.alloc_sbuf_tensor("c_eps", [128, 1], F32)
    c_nhalf = nc.alloc_sbuf_tensor("c_nhalf", [128, 1], F32)
    c_inv = nc.alloc_sbuf_tensor("c_inv", [128, 1], F32)
    s1c = nc.alloc_sbuf_tensor("s1c", [128, HC], F32)
    s2c = nc.alloc_sbuf_tensor("s2c", [128, HC], F32)
    sh1c = nc.alloc_sbuf_tensor("sh1c", [128, HC], F32)
    sh2c = nc.alloc_sbuf_tensor("sh2c", [128, HC], F32)
    s3c = nc.alloc_sbuf_tensor("s3c", [128, KC], F32)
    rowm = nc.alloc_sbuf_tensor("rowm", [1, TA], F32)
    rowq = nc.alloc_sbuf_tensor("rowq", [1, TA], F32)
    rowt = nc.alloc_sbuf_tensor("rowt", [1, TA], F32)
    rowv = nc.alloc_sbuf_tensor("rowv", [1, TA], F32)
    rowr = nc.alloc_sbuf_tensor("rowr", [1, TA], F32)
    rowmr = nc.alloc_sbuf_tensor("rowmr", [1, TA], F32)
    rowfix = nc.alloc_sbuf_tensor("rowfix", [1, TA], F32)

    ARENA_B = 198 * 1024
    arena = nc.alloc_sbuf_tensor("arena", [128, ARENA_B], mybir.dt.uint8)
    abase = nc.lookup_mloc(arena).addr

    offsets = {}

    def place(cursor, name, shape, dtype, align=128):
        nbytes = int(np.prod(shape[1:])) * mybir.dt.size(dtype)
        off = (cursor[0] + align - 1) // align * align
        assert off + nbytes <= ARENA_B, f"arena overflow at {name}: {off + nbytes}"
        cursor[0] = off + nbytes
        offsets[name] = off
        return nc.alloc_sbuf_tensor_at(name, shape, dtype, offset=abase + off)

    def place_at(name, at_name, shape, dtype):
        nbytes = int(np.prod(shape[1:])) * mybir.dt.size(dtype)
        return nc.alloc_sbuf_tensor_at(name, shape, dtype,
                                       offset=abase + offsets[at_name])

    ca = [0]
    T0 = place(ca, "T0", [128, KC, NDIM], BF16)
    T1 = place(ca, "T1", [128, KC, NDIM], BF16)
    Tl0 = place(ca, "Tl0", [128, KC, NDIM], BF16)
    arn_off = ca[0]
    WH = place(ca, "WHb", [128, KC, NDIM], BF16)
    WL = place(ca, "WLb", [128, KC, NDIM], BF16)
    IM0 = place(ca, "IM0b", [128, KC, NDIM], BF16)
    IM1 = place(ca, "IM1b", [128, KC, NDIM], BF16)
    IML = place(ca, "IMLb", [128, KC, NDIM], BF16)
    XTs = [place(ca, "XT0", [128, KC, TA], F32),
           place(ca, "XT1", [128, KC, TA], F32)]
    XNH = place(ca, "XNH", [128, KC, TA], BF16)
    XNL = place(ca, "XNL", [128, KC, TA], BF16)
    LH = place(ca, "LH", [128, KC, TA], BF16)
    LL = place(ca, "LL", [128, KC, TA], BF16)
    P0 = place(ca, "P0", [128, KC, TA], F32)
    P1 = place(ca, "P1", [128, KC, TA], F32)
    tmpA = place(ca, "tmpA", [128, 4, TA], F32)
    tmpB = place(ca, "tmpB", [128, 4, TA], F32)
    tmpC = place(ca, "tmpC", [128, 4, TA], F32)
    sqcs = [place(ca, "sqc0", [128, TA], F32),
            place(ca, "sqc1", [128, TA], F32)]
    xnc = place(ca, "xnc", [128, TA], F32)
    xnc2 = place(ca, "xnc2", [128, TA], F32)
    ptc0 = place(ca, "ptc0", [128, TA], F32)
    ptc1 = place(ca, "ptc1", [128, TA], F32)
    assert ca[0] <= ARENA_B
    NBOUNCE = 3
    wf32s = [place_at(f"wf32{i}", nm, [128, NDIM], F32)
             for i, nm in enumerate(("tmpA", "tmpB", "tmpC"))]
    SLA = [nc.alloc_sbuf_tensor_at(f"SLA{i}", [128, KC, 2, 128], BF16,
                                   offset=abase + offsets["WHb"] + 4096 * i)
           for i in range(3)]
    fold_stg = [nc.alloc_sbuf_tensor_at(f"fhl{i}", [128, 2, 512], BF16,
                                        offset=abase + offsets["XT1"] + 2048 * i)
                for i in range(4)]

    cf = [0]
    XNFHs = [place(cf, "XNFH0", [128, KC, TF], BF16),
             place(cf, "XNFH1", [128, KC, TF], BF16)]
    XNFLs = [place(cf, "XNFL0", [128, KC, TF], BF16),
             place(cf, "XNFL1", [128, KC, TF], BF16)]
    GH = place(cf, "GH", [128, HC, TF], BF16)
    GL = place(cf, "GL", [128, HC, TF], BF16)
    H1H = place(cf, "H1H", [128, HC, TF], BF16)
    H1L = place(cf, "H1L", [128, HC, TF], BF16)
    SLB0 = place(cf, "SLB0", [128, HC, 128], BF16)
    SLB1 = place(cf, "SLB1", [128, HC, 128], BF16)
    SLW0 = place_at("SLW0", "SLB0", [128, KC, 512], BF16)
    SLW1 = place_at("SLW1", "SLB1", [128, KC, 512], BF16)
    tsil = place(cf, "tsil", [128, TF], F32)
    g32 = place(cf, "g32", [128, TF], F32)
    h32 = tsil
    ptf = [place(cf, "ptf0", [128, TF], F32), place(cf, "ptf1", [128, TF], F32)]
    outc0 = place(cf, "outc0", [128, TF], F32)
    outc1 = place(cf, "outc1", [128, TF], F32)
    assert cf[0] <= ARENA_B, cf[0]
    H2H, H2L = GH, GL

    ps = [nc.alloc_psum_tensor(f"ps{i}", [128, 512], F32) for i in range(8)]

    trk = {}
    def T(obj, tag=None):
        key = (id(obj), tag)
        if key not in trk:
            trk[key] = _Trk()
        return trk[key]
    trk[(id(SLW0), None)] = T(SLB0)
    trk[(id(SLW1), None)] = T(SLB1)

    S.op("vector", lambda h: h.memset(ones_col[:], 1.0), writes=[T(ones_col)])
    S.op("vector", lambda h: h.memset(ones_row[:], 1.0), writes=[T(ones_row)])
    S.op("vector", lambda h: h.memset(carry[:], 0.0), writes=[T(carry)])
    S.op("vector", lambda h: h.memset(zeros_sc[:], 0.0), writes=[T(zeros_sc)])
    S.op("vector", lambda h: h.memset(c_eps[:], EPS), writes=[T(c_eps)])
    S.op("vector", lambda h: h.memset(c_nhalf[:], -0.5), writes=[T(c_nhalf)])
    S.op("vector", lambda h: h.memset(c_inv[:], 1.0 / NDIM), writes=[T(c_inv)])
    for dst, src in ((s1c, s1m), (s2c, s2m), (sh1c, sh1m), (sh2c, sh2m), (s3c, s3m)):
        S.op("sync", lambda h, d=dst, s=src: h.dma_start(out=d[:], in_=s),
             writes=[T(dst)], dma=True)

    def emit_ln(t):
        t0 = t * TA
        XT = XTs[t % 2]

        S.op("sync", lambda h, o=t0, X=XT: h.dma_start(
            out=X[:], in_=x_ds[:, o:o + TA].rearrange("(c p) t -> p c t", p=128)),
            writes=[T(XT)], dma=True)

        def mm_sum(h, c, X=XT):
            return h.matmul(ps[4][0:1, 0:TA], ones_col[:], X[:, c, :],
                            start=(c == 0), stop=(c == KC - 1))
        for c in range(KC):
            S.op("tensor", lambda h, c=c, f=mm_sum: f(h, c),
                 reads=[T(XT), T(ones_col)], writes=[T(ps[4])])
        for c in range(KC):
            sqc = sqcs[c % 2]
            S.op("scalar", lambda h, c=c, q=sqc, X=XT: h.activation(q[:], X[:, c, :], AF.Square),
                 reads=[T(XT)], writes=[T(sqc)])
            S.op("tensor", lambda h, c=c, q=sqc: h.matmul(
                ps[5][0:1, 0:TA], ones_col[:], q[:],
                start=(c == 0), stop=(c == KC - 1)),
                reads=[T(sqc), T(ones_col)], writes=[T(ps[5])])

        S.op("scalar", lambda h: h.activation(rowm[:], ps[4][0:1, 0:TA], AF.Copy,
                                              scale=c_inv[0:1, 0:1]),
             reads=[T(ps[4]), T(c_inv)], writes=[T(rowm)])
        S.op("scalar", lambda h: h.activation(rowq[:], ps[5][0:1, 0:TA], AF.Copy,
                                              scale=c_inv[0:1, 0:1]),
             reads=[T(ps[5]), T(c_inv)], writes=[T(rowq)])
        S.op("vector", lambda h: h.tensor_tensor(rowt[:], rowm[:], rowm[:], ALU.mult),
             reads=[T(rowm)], writes=[T(rowt)])
        S.op("vector", lambda h: h.tensor_tensor(rowv[:], rowq[:], rowt[:], ALU.subtract),
             reads=[T(rowq), T(rowt)], writes=[T(rowv)])
        S.op("scalar", lambda h: h.activation(rowt[:], rowv[:], AF.Ln,
                                              bias=c_eps[0:1, 0:1]),
             reads=[T(rowv), T(c_eps)], writes=[T(rowt)])
        S.op("scalar", lambda h: h.activation(rowr[:], rowt[:], AF.Exp,
                                              scale=c_nhalf[0:1, 0:1]),
             reads=[T(rowt), T(c_nhalf)], writes=[T(rowr)])
        S.op("vector", lambda h: h.tensor_tensor(rowmr[:], rowm[:], rowr[:], ALU.mult),
             reads=[T(rowm), T(rowr)], writes=[T(rowmr)])

        S.op("tensor", lambda h: h.matmul(ps[6][:, 0:TA], ones_row[:], rowr[:],
                                          start=True, stop=True),
             reads=[T(rowr), T(ones_row)], writes=[T(ps[6])])
        S.op("tensor", lambda h: h.matmul(ps[7][:, 0:TA], ones_row[:], rowmr[:],
                                          start=True, stop=True),
             reads=[T(rowmr), T(ones_row)], writes=[T(ps[7])])

        for c in range(KC):
            S.op("vector", lambda h, c=c, X=XT: h.tensor_tensor(
                xnc[:], X[:, c, :], ps[6][:, 0:TA], ALU.mult),
                reads=[T(XT), T(ps[6])], writes=[T(xnc)])
            S.op("vector", lambda h, c=c: h.tensor_tensor(
                xnc2[:], xnc[:], ps[7][:, 0:TA], ALU.subtract),
                reads=[T(xnc), T(ps[7])], writes=[T(xnc2)])
            S.op("scalar", lambda h, c=c: h.copy(XNH[:, c, :], xnc2[:]),
                 reads=[T(xnc2)], writes=[T(XNH)])
            S.op("vector", lambda h, c=c: h.tensor_tensor(
                XNL[:, c, :], xnc2[:], XNH[:, c, :], ALU.subtract),
                reads=[T(xnc2), T(XNH)], writes=[T(XNL)])
        S.op("gpsimd", lambda h, o=t0: h.dma_start(
            out=xnh_scr[:, o:o + TA].rearrange("(c p) t -> p c t", p=128), in_=XNH[:]),
            reads=[T(XNH)], writes=[T(xnh_scr)], dma=True)
        S.op("gpsimd", lambda h, o=t0: h.dma_start(
            out=xnl_scr[:, o:o + TA].rearrange("(c p) t -> p c t", p=128), in_=XNL[:]),
            reads=[T(XNL)], writes=[T(xnl_scr)], dma=True)

    emit_ln(0)

    wjobs = []
    for kc in range(KC):
        wjobs.append(("res2", tT_hi[kc * 128:(kc + 1) * 128, :], T0, T1, kc))
    for kc in range(KC):
        wjobs.append(("res1", tT_lo[kc * 128:(kc + 1) * 128, :], Tl0, None, kc))
    for kc in range(KC):
        wjobs.append(("res2", imr_hi[kc * 128:(kc + 1) * 128, :], IM0, IM1, kc))
    for kc in range(KC):
        wjobs.append(("res1", imr_lo[kc * 128:(kc + 1) * 128, :], IML, None, kc))
    def emit_res_jobs(jobs):
        for job in jobs:
            kind, src, d0, d1, kc = job
            i = emit_res_jobs.wi % NBOUNCE
            emit_res_jobs.wi += 1
            wf32 = wf32s[i]
            S.op("sync", lambda h, s=src, b=wf32: h.dma_start(out=b[:], in_=s),
                 writes=[T(wf32)], dma=True)
            S.op("scalar", lambda h, b=wf32, o=d0, k=kc: h.copy(o[:, k, :], b[:]),
                 reads=[T(wf32)], writes=[T(d0)])
            if kind == "res2":
                S.op("vector", lambda h, b=wf32, o0=d0, o1=d1, k=kc: h.tensor_tensor(
                    o1[:, k, :], b[:], o0[:, k, :], ALU.subtract),
                    reads=[T(wf32), T(d0)], writes=[T(d1)])
    emit_res_jobs.wi = 0
    emit_res_jobs(wjobs)

    for w in "kvq":
        for kc in range(KC):
            i = emit_res_jobs.wi % NBOUNCE
            emit_res_jobs.wi += 1
            wf32 = wf32s[i]
            S.op("sync", lambda h, s=worig[w], k=kc, b=wf32: h.dma_start(
                out=b[:], in_=s[k * 128:(k + 1) * 128, :]),
                writes=[T(wf32)], dma=True)
            S.op("scalar", lambda h, b=wf32, k=kc: h.copy(WH[:, k, :], b[:]),
                 reads=[T(wf32)], writes=[T(WH)])
            S.op("vector", lambda h, b=wf32, k=kc: h.tensor_tensor(
                WL[:, k, :], b[:], WH[:, k, :], ALU.subtract),
                reads=[T(wf32), T(WH)], writes=[T(WL)])
        for dc in range(KC):
            for half in range(2):
                bank = ps[(2 * dc + half) % 4]
                def mm_fold(h, dc=dc, half=half, bank=bank):
                    ins = None
                    first = True
                    for la, lb in ((WH, T0), (WL, T0), (WH, T1), (WH, Tl0)):
                        for kc2 in range(KC):
                            ins = h.matmul(
                                bank[:],
                                la[:, kc2, dc * 128:(dc + 1) * 128],
                                lb[:, kc2, half * 512:half * 512 + 512],
                                start=first,
                                stop=(la is WH and lb is Tl0 and kc2 == KC - 1))
                            first = False
                    return ins
                S.op("tensor", mm_fold,
                     reads=[T(WH), T(WL), T(T0), T(T1), T(Tl0)], writes=[T(bank)])
                fhl = fold_stg[(2 * dc + half) % 4]
                S.op("scalar", lambda h, bank=bank, o=fhl: h.copy(o[:, 0, :], bank[:]),
                     reads=[T(bank)], writes=[T(fhl, "hi")])
                S.op("vector", lambda h, bank=bank, o=fhl: h.tensor_tensor(
                    o[:, 1, :], bank[:], o[:, 0, :], ALU.subtract),
                    reads=[T(bank), T(fhl, "hi")], writes=[T(fhl, "lo")])
                S.op("gpsimd", lambda h, d=wl_scr[w], dc=dc, half=half, o=fhl: h.dma_start(
                    out=d[4 * half:4 * half + 4, dc * 128:(dc + 1) * 128, :, :]
                        .rearrange("c d l m -> d l c m"),
                    in_=o[:].rearrange("d l (c m) -> d l c m", c=4)),
                    reads=[T(fhl, "hi"), T(fhl, "lo")], writes=[T(wl_scr[w])], dma=True)

    S.set_fence(S.fence_now())

    for name, src, dst, nch, inner in (
            ("cw1", w1c, w1b, KC, NDIM * 512), ("cw2", w2c, w2b, KC, NDIM * 512),
            ("ch1", h1c, h1b, HC, HID * 128), ("ch2", h2c, h2b, HC, HID * 128),
            ("cw3", w3c, w3b, KC, HID * 128)):
        step = max(1, (4 << 20) // (inner * 4))
        for c0 in range(0, nch, step):
            c1 = min(nch, c0 + step)
            S.op("gpsimd",
                 lambda h, s=src, d=dst, a=c0, b=c1: h.dma_start(out=d[a:b], in_=s[a:b]),
                 writes=[T(dst)], dma=True, dma_sem=name)

    for t in range(NT_A):
        t0 = t * TA
        XT = XTs[t % 2]

        def proj(wname, dstplane):
            for rc in range(KC):
                sl = SLA[rc % 3]
                S.op("sync", lambda h, s=wl_scr[wname], rc=rc, d=sl: h.dma_start(
                    out=d[:], in_=s[rc].rearrange("(c p) l m -> p c l m", p=128)),
                    reads=[T(wl_scr[wname])], writes=[T(sl)], dma=True)
                bank = ps[rc % 4]
                def mm_proj(h, rc=rc, bank=bank, sl=sl):
                    ins = None
                    for li, lb in ((0, XNH), (0, XNL), (1, XNH)):
                        for kc2 in range(KC):
                            ins = h.matmul(
                                bank[:, 0:TA],
                                sl[:, kc2, li, :],
                                lb[:, kc2, :],
                                start=(li == 0 and lb is XNH and kc2 == 0),
                                stop=(li == 1 and kc2 == KC - 1))
                    return ins
                S.op("tensor", mm_proj,
                     reads=[T(sl), T(XNH), T(XNL)], writes=[T(bank)])
                S.op("scalar", lambda h, rc=rc, bank=bank, d=dstplane: h.copy(
                    d[:, rc, :], bank[:, 0:TA]),
                    reads=[T(bank)], writes=[T(dstplane)])

        proj("k", P0)
        proj("v", P1)

        RE = (slice(None), slice(0, 4), slice(None))
        IM = (slice(None), slice(4, 8), slice(None))
        S.op("vector", lambda h: h.tensor_tensor(
            rowfix[:], P0[0:1, 4, :], P1[0:1, 4, :], ALU.mult),
            reads=[T(P0), T(P1)], writes=[T(rowfix)])
        S.op("vector", lambda h: h.tensor_tensor(tmpA[:], P0[RE], P1[RE], ALU.mult),
             reads=[T(P0), T(P1)], writes=[T(tmpA)])
        S.op("vector", lambda h: h.tensor_tensor(tmpB[:], P0[IM], P1[IM], ALU.mult),
             reads=[T(P0), T(P1)], writes=[T(tmpB)])
        S.op("vector", lambda h: h.tensor_tensor(tmpC[:], tmpA[:], tmpB[:], ALU.subtract),
             reads=[T(tmpA), T(tmpB)], writes=[T(tmpC)])
        S.op("vector", lambda h: h.tensor_tensor(
            tmpC[0:1, 0, :], P0[0:1, 0, :], P1[0:1, 0, :], ALU.mult),
            reads=[T(P0), T(P1)], writes=[T(tmpC)])
        S.op("vector", lambda h: h.tensor_tensor(tmpA[:], P0[RE], P1[IM], ALU.mult),
             reads=[T(P0), T(P1)], writes=[T(tmpA)])
        S.op("vector", lambda h: h.tensor_tensor(tmpB[:], P0[IM], P1[RE], ALU.mult),
             reads=[T(P0), T(P1)], writes=[T(tmpB)])
        S.op("vector", lambda h: h.tensor_tensor(P0[IM], tmpA[:], tmpB[:], ALU.add),
             reads=[T(tmpA), T(tmpB)], writes=[T(P0)])
        S.op("scalar", lambda h: h.copy(P0[0:1, 4, :], rowfix[:]),
             reads=[T(rowfix)], writes=[T(P0)])
        for c in range(4):
            S.op("vector", lambda h, c=c: h.tensor_tensor_scan(
                P1[:, c, :], tmpC[:, c, :], zeros_sc[:], carry[:, c:c + 1],
                ALU.add, ALU.add),
                reads=[T(tmpC), T(zeros_sc), T(carry)], writes=[T(P1)])
        for c in range(4):
            S.op("vector", lambda h, c=c: h.tensor_tensor_scan(
                P1[:, 4 + c, :], P0[:, 4 + c, :], zeros_sc[:], carry[:, 4 + c:5 + c],
                ALU.add, ALU.add),
                reads=[T(P0), T(zeros_sc), T(carry)], writes=[T(P1)])
        S.op("vector", lambda h: h.tensor_copy(carry[:, 0:KC], P1[:, :, TA - 1]),
             reads=[T(P1)], writes=[T(carry)])

        proj("q", P0)

        S.op("vector", lambda h: h.tensor_tensor(
            rowfix[:], P1[0:1, 4, :], P0[0:1, 4, :], ALU.mult),
            reads=[T(P0), T(P1)], writes=[T(rowfix)])
        S.op("vector", lambda h: h.tensor_tensor(tmpA[:], P1[RE], P0[RE], ALU.mult),
             reads=[T(P0), T(P1)], writes=[T(tmpA)])
        S.op("vector", lambda h: h.tensor_tensor(tmpB[:], P1[IM], P0[IM], ALU.mult),
             reads=[T(P0), T(P1)], writes=[T(tmpB)])
        S.op("vector", lambda h: h.tensor_tensor(tmpC[:], tmpA[:], tmpB[:], ALU.add),
             reads=[T(tmpA), T(tmpB)], writes=[T(tmpC)])
        S.op("vector", lambda h: h.tensor_tensor(
            tmpC[0:1, 0, :], P1[0:1, 0, :], P0[0:1, 0, :], ALU.mult),
            reads=[T(P0), T(P1)], writes=[T(tmpC)])
        S.op("vector", lambda h: h.tensor_tensor(tmpA[:], P1[IM], P0[RE], ALU.mult),
             reads=[T(P0), T(P1)], writes=[T(tmpA)])
        S.op("vector", lambda h: h.tensor_tensor(tmpB[:], P1[RE], P0[IM], ALU.mult),
             reads=[T(P0), T(P1)], writes=[T(tmpB)])
        S.op("vector", lambda h: h.tensor_tensor(P1[IM], tmpA[:], tmpB[:], ALU.subtract),
             reads=[T(tmpA), T(tmpB)], writes=[T(P1)])
        S.op("scalar", lambda h: h.copy(P1[0:1, 4, :], rowfix[:]),
             reads=[T(rowfix)], writes=[T(P1)])

        S.op("scalar", lambda h: h.copy(LH[:, 0:4, :], tmpC[:]),
             reads=[T(tmpC)], writes=[T(LH, "lo")])
        S.op("vector", lambda h: h.tensor_tensor(
            LL[:, 0:4, :], tmpC[:], LH[:, 0:4, :], ALU.subtract),
            reads=[T(tmpC), T(LH, "lo")], writes=[T(LL, "lo")])
        S.op("scalar", lambda h: h.copy(LH[:, 4:8, :], P1[IM]),
             reads=[T(P1)], writes=[T(LH, "hi")])
        S.op("vector", lambda h: h.tensor_tensor(
            LL[:, 4:8, :], P1[IM], LH[:, 4:8, :], ALU.subtract),
            reads=[T(P1), T(LH, "hi")], writes=[T(LL, "hi")])

        if t + 1 < NT_A:
            emit_ln(t + 1)

        for dc in range(KC):
            bank = ps[dc % 4]
            def mm_inv_half(h, dc=dc, bank=bank, klo=0, khi=4, first=True, last=False):
                ins = None
                for la, lb in ((IM0, LH), (IM1, LH), (IM0, LL), (IML, LH)):
                    for kc2 in range(klo, khi):
                        ins = h.matmul(
                            bank[:, 0:TA],
                            la[:, kc2, dc * 128:(dc + 1) * 128],
                            lb[:, kc2, :],
                            start=first,
                            stop=(last and la is IML and kc2 == khi - 1))
                        first = False
                return ins
            import functools as _ft
            S.op("tensor", _ft.partial(mm_inv_half, klo=0, khi=4, first=True, last=False),
                 reads=[T(IM0), T(IM1), T(IML), T(LH, "lo"), T(LL, "lo")],
                 writes=[T(bank)])
            S.op("tensor", _ft.partial(mm_inv_half, klo=4, khi=8, first=False, last=True),
                 reads=[T(IM0), T(IM1), T(IML), T(LH, "hi"), T(LL, "hi")],
                 writes=[T(bank)])
            pt = ptc0 if dc % 2 == 0 else ptc1
            S.op("vector", lambda h, dc=dc, bank=bank, pt=pt, X=XT: h.tensor_tensor(
                pt[:], bank[:, 0:TA], X[:, dc, :], ALU.add),
                reads=[T(bank), T(XT)], writes=[T(pt)])
            S.op("gpsimd", lambda h, dc=dc, o=t0, pt=pt: h.dma_start(
                out=part_scr[dc * 128:(dc + 1) * 128, o:o + TA], in_=pt[:]),
                reads=[T(pt)], writes=[T(part_scr)], dma=True)

    S.set_fence(S.fence_now())
    for t in range(NT_F):
        t0 = t * TF
        XNFH, XNFL = XNFHs[t % 2], XNFLs[t % 2]
        S.op("sync", lambda h, o=t0, D=XNFH: h.dma_start(
            out=D[:], in_=xnh_scr[:, o:o + TF].rearrange("(c p) t -> p c t", p=128)),
            reads=[T(xnh_scr)], writes=[T(XNFH)], dma=True)
        S.op("sync", lambda h, o=t0, D=XNFL: h.dma_start(
            out=D[:], in_=xnl_scr[:, o:o + TF].rearrange("(c p) t -> p c t", p=128)),
            reads=[T(xnl_scr)], writes=[T(XNFL)], dma=True)

        for oq in range(KC):
            S.op("sync", lambda h, oq=oq: h.dma_start(
                out=SLW0[:], in_=w1b[oq].rearrange("(c p) m -> p c m", p=128)),
                reads=[T(w1b)], writes=[T(SLW0)], dma=True)
            S.op("sync", lambda h, oq=oq: h.dma_start(
                out=SLW1[:], in_=w2b[oq].rearrange("(c p) m -> p c m", p=128)),
                reads=[T(w2b)], writes=[T(SLW1)], dma=True)
            for ol in range(4):
                oc = oq * 4 + ol
                b1 = ps[(2 * oc) % 4]
                b2 = ps[(2 * oc + 1) % 4]
                def mm_w12(h, ol=ol, sl=None, bank=None, XH=XNFH, XL=XNFL):
                    ins = None
                    first = True
                    for lb in (XH, XL):
                        for kc2 in range(KC):
                            ins = h.matmul(bank[:, 0:TF],
                                           sl[:, kc2, ol * 128:(ol + 1) * 128],
                                           lb[:, kc2, :],
                                           start=first,
                                           stop=(lb is XL and kc2 == KC - 1))
                            first = False
                    return ins
                import functools
                S.op("tensor", functools.partial(mm_w12, sl=SLW0, bank=b1),
                     reads=[T(SLW0), T(XNFH), T(XNFL)], writes=[T(b1)])
                S.op("tensor", functools.partial(mm_w12, sl=SLW1, bank=b2),
                     reads=[T(SLW1), T(XNFH), T(XNFL)], writes=[T(b2)])
                S.op("scalar", lambda h, oc=oc, bank=b1: h.activation(
                    tsil[:], bank[:, 0:TF], AF.Silu, scale=s1c[:, oc:oc + 1]),
                    reads=[T(b1), T(s1c)], writes=[T(tsil)])
                S.op("vector", lambda h, oc=oc, bank=b2: h.scalar_tensor_tensor(
                    g32[:], bank[:, 0:TF], s2c[:, oc:oc + 1], tsil[:],
                    ALU.mult, ALU.mult),
                    reads=[T(b2), T(s2c), T(tsil)], writes=[T(g32)])
                S.op("vector", lambda h, oc=oc: h.tensor_copy(GH[:, oc, :], g32[:]),
                     reads=[T(g32)], writes=[T(GH)])
                S.op("vector", lambda h, oc=oc: h.tensor_tensor(
                    GL[:, oc, :], g32[:], GH[:, oc, :], ALU.subtract),
                    reads=[T(g32), T(GH)], writes=[T(GL)])

        def hidden_layer(wsrc, sc_col, inH, inL, outH, outL):
            for oc in range(HC):
                sl = SLB0 if oc % 2 == 0 else SLB1
                bank = ps[oc % 4]
                S.op("sync", lambda h, oc=oc, sl=sl, w=wsrc: h.dma_start(
                    out=sl[:], in_=w[oc].rearrange("(c p) m -> p c m", p=128)),
                    reads=[T(wsrc)], writes=[T(sl)], dma=True)
                def mm_h(h, oc=oc, sl=sl, bank=bank, inH=inH, inL=inL):
                    ins = None
                    first = True
                    for lb in (inH, inL):
                        for kc2 in range(HC):
                            ins = h.matmul(bank[:, 0:TF], sl[:, kc2, :], lb[:, kc2, :],
                                           start=first,
                                           stop=(lb is inL and kc2 == HC - 1))
                            first = False
                    return ins
                S.op("tensor", mm_h, reads=[T(sl), T(inH), T(inL)], writes=[T(bank)])
                S.op("scalar", lambda h, oc=oc, bank=bank, sc=sc_col: h.activation(
                    h32[:], bank[:, 0:TF], AF.Gelu, scale=sc[:, oc:oc + 1]),
                    reads=[T(bank), T(sc_col)], writes=[T(h32)])
                S.op("vector", lambda h, oc=oc, oh=outH: h.tensor_copy(
                    oh[:, oc, :], h32[:]),
                    reads=[T(h32)], writes=[T(outH)])
                S.op("vector", lambda h, oc=oc, oh=outH, ol=outL: h.tensor_tensor(
                    ol[:, oc, :], h32[:], oh[:, oc, :], ALU.subtract),
                    reads=[T(h32), T(outH)], writes=[T(outL)])

        hidden_layer(h1b, sh1c, GH, GL, H1H, H1L)
        hidden_layer(h2b, sh2c, H1H, H1L, H2H, H2L)

        for dc in range(KC):
            sl = SLB0 if dc % 2 == 0 else SLB1
            bank = ps[dc % 4]
            S.op("sync", lambda h, dc=dc, sl=sl: h.dma_start(
                out=sl[:], in_=w3b[dc].rearrange("(c p) m -> p c m", p=128)),
                reads=[T(w3b)], writes=[T(sl)], dma=True)
            def mm_w3(h, dc=dc, sl=sl, bank=bank):
                ins = None
                first = True
                for lb in (H2H, H2L):
                    for kc2 in range(HC):
                        ins = h.matmul(bank[:, 0:TF], sl[:, kc2, :], lb[:, kc2, :],
                                       start=first, stop=(lb is H2L and kc2 == HC - 1))
                        first = False
                return ins
            S.op("tensor", mm_w3, reads=[T(sl), T(H2H), T(H2L)], writes=[T(bank)])
            ptb = ptf[dc % 2]
            S.op("sync", lambda h, dc=dc, o=t0, D=ptb: h.dma_start(
                out=D[:], in_=part_scr[dc * 128:(dc + 1) * 128, o:o + TF]),
                reads=[T(part_scr)], writes=[T(ptb)], dma=True)
            oc_buf = outc0 if dc % 2 == 0 else outc1
            S.op("vector", lambda h, dc=dc, bank=bank, ob=oc_buf, D=ptb: h.scalar_tensor_tensor(
                ob[:], bank[:, 0:TF], s3c[:, dc:dc + 1], D[:],
                ALU.mult, ALU.add),
                reads=[T(bank), T(s3c), T(ptb)], writes=[T(oc_buf)])
            S.op("gpsimd", lambda h, dc=dc, o=t0, ob=oc_buf: h.dma_start(
                out=y[dc * 128:(dc + 1) * 128, o:o + TF], in_=ob[:]),
                reads=[T(oc_buf)], writes=[], dma=True)

    finals = dict(S.counts)

    import contextlib
    with contextlib.ExitStack() as stack:
        sems = {name: stack.enter_context(nc.semaphore("sem_" + name))
                for name in S.counts}
        block = stack.enter_context(nc.Block())
        S.emit_all(block, sems)

        @block.sync
        def _(sync):
            for name, val in finals.items():
                if val > 0:
                    sync.wait_ge(sems[name], val)

    return nc


def _prep_inputs(inputs):
    _dft_constants()
    x = np.asarray(inputs["L0_attn_x"])
    com = {
        "tT_hi": _CACHE["tT_hi"], "tT_lo": _CACHE["tT_lo"],
        "imr_hi": _CACHE["im_hi"], "imr_lo": _CACHE["im_lo"],
    }
    for w in "qkv":
        com[f"w{w}o"] = np.ascontiguousarray(np.asarray(inputs[f"L0_attn_W{w}"]))
    def chunked(a, cw=128):
        O, K = a.shape
        return np.ascontiguousarray(
            a.T.reshape(K, O // cw, cw).transpose(1, 0, 2))
    com["w1c"] = chunked(np.asarray(inputs["L0_ffn_w1_w"]), 512)
    com["w2c"] = chunked(np.asarray(inputs["L0_ffn_w2_w"]), 512)
    com["h1c"] = chunked(np.asarray(inputs["L0_ffn_h1_w"]))
    com["h2c"] = chunked(np.asarray(inputs["L0_ffn_h2_w"]))
    com["w3c"] = chunked(np.asarray(inputs["L0_ffn_w3_w"]))
    def colmaj(v):
        v = np.asarray(v)
        return np.ascontiguousarray(v.reshape(-1, 128).T)
    com["s1m"] = colmaj(inputs["L0_ffn_w1_s"])
    com["s2m"] = colmaj(inputs["L0_ffn_w2_s"])
    com["sh1m"] = colmaj(inputs["L0_ffn_h1_s"])
    com["sh2m"] = colmaj(inputs["L0_ffn_h2_s"])
    com["s3m"] = colmaj(inputs["L0_ffn_w3_s"])
    in_maps = []
    for b in range(8):
        m = dict(com)
        m["x_ds"] = np.ascontiguousarray(x[b].T)
        in_maps.append(m)
    return in_maps


def kernel(**inputs):
    in_maps = _prep_inputs(inputs)
    if "nc" not in _CACHE:
        _CACHE["nc"] = build_nc()
    res = run_bass_kernel_spmd(_CACHE["nc"], in_maps, core_ids=list(range(8)))
    out = np.stack([np.asarray(r["y"]).T for r in res.results], axis=0)
    return np.ascontiguousarray(out.astype(np.float32, copy=False))
